# version 02
# baseline (speedup 1.0000x reference)
from contextlib import ExitStack

import numpy as np
import concourse.bass as bass
import concourse.mybir as mybir
from concourse.bass_core_rust import IndirectOffsetOnAxis
from concourse.bass_utils import run_bass_kernel_spmd

F32 = mybir.dt.float32
BF16 = mybir.dt.bfloat16
I32 = mybir.dt.int32
AF = mybir.ActivationFunctionType
ALU = mybir.AluOpType
AX = mybir.AxisListType

W = 8
B = 2
S = 8192
H = 2048
TOK = B * S
CH = TOK // W
SCALE = 1.0 / float(np.sqrt(np.float32(128.0)))
CMARG = 1.0202

ENGINES = ("sync", "vector", "scalar", "tensor", "gpsimd")


class KB:

    def __init__(self, nc):
        self.nc = nc
        self.q = {e: [] for e in ENGINES}

    def mksem(self, stack, name):
        h = stack.enter_context(self.nc.semaphore(name))
        return {"h": h, "n": 0, "name": name}

    def on(self, eng, fn):
        self.q[eng].append(fn)

    def wait(self, eng, sem, n):
        if n <= 0:
            return
        h = sem["h"]
        self.on(eng, lambda e, h=h, n=n: e.wait_ge(h, n))

    def op(self, eng, fn, mark=None):
        if mark is None:
            self.on(eng, fn)
            return None
        mark["n"] += 1
        h, tgt = mark["h"], mark["n"]
        self.on(eng, lambda e, fn=fn, h=h: fn(e).then_inc(h, 1))
        return tgt

    def dma(self, eng, out, in_, sem, transpose=False):
        sem["n"] += 16
        h, tgt = sem["h"], sem["n"]

        def fn(e, out=out, in_=in_, h=h):
            if transpose:
                return e.dma_start_transpose(out, in_).then_inc(h, 16)
            return e.dma_start(out=out, in_=in_).then_inc(h, 16)

        self.on(eng, fn)
        return tgt

    def idma(self, out, in_, off_ap, sem):
        sem["n"] += 16
        h, tgt = sem["h"], sem["n"]
        self.on("gpsimd", lambda e, out=out, in_=in_, off=off_ap, h=h:
                e.indirect_dma_start(
                    out=out, out_offset=None, in_=in_,
                    in_offset=IndirectOffsetOnAxis(off, axis=0)).then_inc(h, 16))
        return tgt

    def barrier(self, sem):
        sem["n"] += len(ENGINES)
        tgt, h = sem["n"], sem["h"]
        for eng in ENGINES:
            self.on(eng, lambda e, h=h: e.nop().then_inc(h, 1))
            self.on(eng, lambda e, h=h, t=tgt: e.wait_ge(h, t))

    def emit(self, blk):
        for eng in ENGINES:
            fns = self.q[eng]

            def body(e, fns=fns):
                for f in fns:
                    f(e)

            getattr(blk, eng)(body)


def build(debug=False):
    nc = bass.Bass("TRN2", target_bir_lowering=False)

    x_own = nc.dram_tensor("x_own", [CH, H], F32, kind="ExternalInput").ap()
    x_own_t = nc.dram_tensor("x_own_t", [H, CH], F32, kind="ExternalInput").ap()
    wq_t = nc.dram_tensor("wq_t", [H, 768], F32, kind="ExternalInput").ap()
    wo_t = nc.dram_tensor("wo_t", [H, H], F32, kind="ExternalInput").ap()
    wf_t = nc.dram_tensor("wf_t", [H, H], F32, kind="ExternalInput").ap()
    fb_in = nc.dram_tensor("fb", [128, 16], F32, kind="ExternalInput").ap()
    gidx = nc.dram_tensor("gidx", [128, 16, 8], I32, kind="ExternalInput").ap()
    y_t = nc.dram_tensor("y_t", [H, CH], F32, kind="ExternalOutput").ap()
    if debug:
        o_q = nc.dram_tensor("o_q", [128, S], BF16, kind="ExternalOutput").ap()
        o_k = nc.dram_tensor("o_k", [128, S], BF16, kind="ExternalOutput").ap()
        o_vt = nc.dram_tensor("o_vt", [128, 64, 132], BF16, kind="ExternalOutput").ap()
        o_stage = nc.dram_tensor("o_stage", [128, 64, 256], BF16, kind="ExternalOutput").ap()
        o_negc = nc.dram_tensor("o_negc", [128, 1], F32, kind="ExternalOutput").ap()
        o_xnd = nc.dram_tensor("o_xnd", [CH, H], BF16, kind="ExternalOutput").ap()
        o_xnall = nc.dram_tensor("o_xnall", [128, H], BF16, kind="ExternalOutput").ap()
        o_atd = nc.dram_tensor("o_atd", [128, 256], BF16, kind="ExternalOutput").ap()
        o_atall = nc.dram_tensor("o_atall", [128, 256], BF16, kind="ExternalOutput").ap()
        o_attnT = nc.dram_tensor("o_attnT", [128, 16, 2048], BF16, kind="ExternalOutput").ap()
        o_xnTo = nc.dram_tensor("o_xnTo", [128, 16, 2048], BF16, kind="ExternalOutput").ap()
        o_gT = nc.dram_tensor("o_gT", [128, 2, 2048], BF16, kind="ExternalOutput").ap()

    xn_d = nc.dram_tensor("xn_d", [CH, H], BF16, kind="Internal").ap()
    xn_all = nc.dram_tensor("xn_all", [TOK, H], BF16, kind="Internal").ap()
    at_d = nc.dram_tensor("at_d", [TOK, 256], BF16, kind="Internal").ap()
    at_all = nc.dram_tensor("at_all", [W * TOK, 256], BF16, kind="Internal").ap()

    kb = KB(nc)
    st = ExitStack()

    dS = kb.mksem(st, "dS")
    dG = kb.mksem(st, "dG")
    cc = kb.mksem(st, "cc")
    pT = kb.mksem(st, "pT")
    pV = kb.mksem(st, "pV")
    pC = kb.mksem(st, "pC")
    pP = kb.mksem(st, "pP")
    bar = kb.mksem(st, "bar")

    idb = st.enter_context(nc.sbuf_tensor("idb", [128, 128], BF16))
    onesb = st.enter_context(nc.sbuf_tensor("onesb", [128, 128], BF16))
    idio = st.enter_context(nc.sbuf_tensor("idio", [128, 128], F32))
    epsb = st.enter_context(nc.sbuf_tensor("epsb", [128, 1], F32))
    gix = st.enter_context(nc.sbuf_tensor("gix", [128, 16, 8], I32))
    fbs = st.enter_context(nc.sbuf_tensor("fbs", [128, 16], F32))

    t_gix = kb.dma("sync", gix[:], gidx, dS)
    kb.dma("sync", fbs[:], fb_in, dS)
    kb.op("gpsimd", lambda e: e.iota(
        idio[:], pattern=[[1, 128]], base=0, channel_multiplier=-1,
        allow_small_or_imprecise_dtypes=True), mark=pP)
    kb.wait("vector", pP, pP["n"])
    kb.op("vector", lambda e: e.tensor_single_scalar(idb[:], idio[:], 0.0, ALU.is_equal))
    kb.op("vector", lambda e: e.memset(onesb[:], 1.0))
    m_eps = kb.op("vector", lambda e: e.memset(epsb[:], 1e-5), mark=pV)

    p3 = ExitStack()
    wqkvT = p3.enter_context(nc.sbuf_tensor("wqkvT", [128, 16, 768], BF16))
    t_wq = kb.dma("gpsimd", wqkvT[:],
                  wq_t.rearrange("(ic p) o -> p ic o", p=128), dG)

    with ExitStack() as p1:
        xof = p1.enter_context(nc.sbuf_tensor("xof", [128, 2, H], F32))
        xnb = p1.enter_context(nc.sbuf_tensor("xnb", [128, 2, H], BF16))
        bnst = p1.enter_context(nc.sbuf_tensor("bnst", [128, 24], F32))
        bnag = p1.enter_context(nc.sbuf_tensor("bnag", [128, 2], F32))
        lnstd = p1.enter_context(nc.sbuf_tensor("lnstd", [128, 1], F32))
        lnr = p1.enter_context(nc.sbuf_tensor("lnr", [128, 1], F32))

        xn_dma = [0, 0]
        xf_rd = [0, 0]
        m_xn = 0
        kb.wait("scalar", pV, m_eps)
        for t in range(16):
            par = t % 2
            kb.wait("sync", pV, xf_rd[par])
            t_x = kb.dma("sync", xof[:, par, :], x_own[t * 128:(t + 1) * 128, :], dS)
            kb.wait("vector", dS, t_x)
            kb.wait("vector", pV, m_xn)
            for c4 in range(3):
                kb.op("vector", lambda e, par=par, c4=c4: e.bn_stats(
                    bnst[:, c4 * 6:(c4 + 1) * 6], xof[:, par, c4 * 512:(c4 + 1) * 512]))
            m_st = kb.op("vector", lambda e, par=par: e.bn_stats(
                bnst[:, 18:24], xof[:, par, 1536:2048]), mark=pV)
            kb.wait("vector", pV, m_st)
            m_bn = kb.op("vector", lambda e: e.bn_aggr(bnag[:], bnst[:]), mark=pV)
            kb.wait("scalar", pV, m_bn)
            m_sq = kb.op("scalar", lambda e: e.activation(
                lnstd[:], bnag[:, 1:2], AF.Sqrt, bias=epsb[:], scale=1.0), mark=pC)
            kb.wait("vector", pC, m_sq)
            m_rc = kb.op("vector", lambda e: e.reciprocal(lnr[:], lnstd[:]), mark=pV)
            kb.wait("vector", pV, m_rc)
            kb.wait("vector", dS, xn_dma[par])
            m_xn = kb.op("vector", lambda e, par=par: e.tensor_scalar(
                xnb[:, par, :], xof[:, par, :], bnag[:, 0:1], lnr[:],
                ALU.subtract, ALU.mult), mark=pV)
            xf_rd[par] = m_xn
            kb.wait("sync", pV, m_xn)
            xn_dma[par] = kb.dma("sync", xn_d[t * 128:(t + 1) * 128, :],
                                 xnb[:, par, :], dS)
        t_xn_done = dS["n"]

    kb.wait("gpsimd", dS, t_xn_done)
    kb.op("gpsimd", lambda e: e.collective_compute(
        "AllGather", ALU.bypass, replica_groups=[list(range(W))],
        ins=[xn_d.opt()], outs=[xn_all.opt()]), mark=cc)
    cc_xn = cc["n"]

    stage = p3.enter_context(nc.sbuf_tensor("stage", [128, 64, 256], BF16))
    xnT = p3.enter_context(nc.sbuf_tensor("xnT", [128, 2, 16, 1024], BF16))
    qth = p3.enter_context(nc.sbuf_tensor("qth", [128, S], BF16))
    kth = p3.enter_context(nc.sbuf_tensor("kth", [128, S], BF16))
    vtih = p3.enter_context(nc.sbuf_tensor("vtih", [128, S], BF16))
    vth = p3.enter_context(nc.sbuf_tensor("vth", [128, 64, 132], BF16))
    Ebuf = p3.enter_context(nc.sbuf_tensor("Ebuf", [128, 2, 2048], BF16))
    rq = p3.enter_context(nc.sbuf_tensor("rq", [128, 1], F32))
    rk = p3.enter_context(nc.sbuf_tensor("rk", [128, 1], F32))
    rstat = p3.enter_context(nc.sbuf_tensor("rstat", [128, 8], F32))
    negc = p3.enter_context(nc.sbuf_tensor("negc", [128, 1], F32))
    rlq = p3.enter_context(nc.sbuf_tensor("rlq", [128, 4], F32))

    kb.op("vector", lambda e: e.memset(vth[:, :, 128:132], 0.0))
    m_vones = kb.op("vector", lambda e: e.memset(vth[:, :, 128:129], 1.0), mark=pV)

    kb.wait("sync", cc, cc_xn)

    xnT_rd = [0, 0]
    stage_dma = 0
    st_wr = 0
    ops_free = 0
    exp_done = 0
    pv_done_for_E = [0, 0]

    for b in range(B):
        for hl in range(2):
            with ExitStack() as pq:
                qkvps = pq.enter_context(nc.psum_tensor(f"qkvps{b}_{hl}", [128, 2048], F32))
                kb.wait("tensor", pV, ops_free)
                ev_done = [0, 0]
                for m in range(8):
                    par = m % 2
                    kb.wait("sync", pT, xnT_rd[par])
                    r0 = b * S + m * 1024
                    for ic in range(16):
                        t_ld = kb.dma(
                            "sync", xnT[:, par, ic, :],
                            xn_all[r0:r0 + 1024, ic * 128:(ic + 1) * 128],
                            dS, transpose=True)
                    if b == 0 and hl == 0 and m == 0:
                        kb.wait("tensor", dG, t_wq)
                    kb.wait("tensor", dS, t_ld)
                    for j, oc in enumerate((hl, 2 + hl, 4 + hl)):
                        pp = j % 2
                        kb.wait("tensor", pV, ev_done[pp])
                        for ic in range(16):
                            for tb in range(2):
                                fn = lambda e, ic=ic, tb=tb, par=par, pp=pp, oc=oc: \
                                    e.matmul(
                                        qkvps[:, pp * 1024 + tb * 512:
                                              pp * 1024 + (tb + 1) * 512],
                                        wqkvT[:, ic, oc * 128:(oc + 1) * 128],
                                        xnT[:, par, ic, tb * 512:(tb + 1) * 512],
                                        start=(ic == 0), stop=(ic == 15))
                                if ic == 15 and tb == 1:
                                    m_mm = kb.op("tensor", fn, mark=pT)
                                else:
                                    kb.op("tensor", fn)
                        if j == 2:
                            xnT_rd[par] = m_mm
                        dst = (qth, kth, vtih)[j]
                        kb.wait("vector", pT, m_mm)
                        ev_done[pp] = kb.op(
                            "vector",
                            lambda e, dst=dst, m=m, pp=pp: e.tensor_copy(
                                dst[:, m * 1024:(m + 1) * 1024],
                                qkvps[:, pp * 1024:(pp + 1) * 1024]),
                            mark=pV)
                m_qkv_ev = pV["n"]

            with ExitStack() as pv_:
                tpsV = pv_.enter_context(nc.psum_tensor(f"tpsV{b}_{hl}", [128, 2, 1024], BF16))
                kb.wait("tensor", pV, m_qkv_ev)
                if b == 0 and hl == 0:
                    kb.wait("tensor", pV, m_vones)
                tr_ev = [0, 0]
                for grp in range(16):
                    gp = grp % 2
                    kb.wait("tensor", pV, tr_ev[gp])
                    for i in range(4):
                        fn = lambda e, grp=grp, i=i, gp=gp: e.transpose(
                            tpsV[:, gp, i * 128:(i + 1) * 128],
                            vtih[:, (grp * 4 + i) * 128:(grp * 4 + i + 1) * 128],
                            idb[:])
                        if i == 3:
                            m_tr = kb.op("tensor", fn, mark=pT)
                        else:
                            kb.op("tensor", fn)
                    kb.wait("vector", pT, m_tr)
                    tr_ev[gp] = kb.op(
                        "vector",
                        lambda e, grp=grp, gp=gp: e.tensor_copy(
                            vth[:, grp * 4:(grp + 1) * 4, 0:128],
                            tpsV[:, gp, 0:512].rearrange("p (k d) -> p k d", k=4)),
                        mark=pV)
                m_vt = pV["n"]

            with ExitStack() as pa:
                sps = pa.enter_context(nc.psum_tensor(f"sps{b}_{hl}", [128, 2048], F32))
                ops = [pa.enter_context(nc.psum_tensor(f"ops{b}_{hl}_{i}", [128, 512], F32))
                       for i in range(4)]

                kb.wait("tensor", pV, m_vt)
                kb.wait("scalar", pV, m_qkv_ev)
                kb.wait("scalar", pT, max(pv_done_for_E))
                m_red = 0
                for which, src in enumerate((qth, kth)):
                    for chn in range(4):
                        m_sq2 = kb.op("scalar", lambda e, src=src, chn=chn: e.activation(
                            Ebuf[:, 0, :], src[:, chn * 2048:(chn + 1) * 2048],
                            AF.Square), mark=pC)
                        kb.wait("tensor", pC, m_sq2)
                        kb.wait("tensor", pV, m_red)
                        for c4 in range(4):
                            fn = lambda e, c4=c4: e.matmul(
                                sps[:, c4 * 512:(c4 + 1) * 512], onesb[:],
                                Ebuf[:, 0, c4 * 512:(c4 + 1) * 512],
                                start=True, stop=True)
                            if c4 == 3:
                                m_nq = kb.op("tensor", fn, mark=pT)
                            else:
                                kb.op("tensor", fn)
                        kb.wait("vector", pT, m_nq)
                        idx = which * 4 + chn
                        m_red = kb.op("vector", lambda e, idx=idx: e.reduce_max(
                            rstat[:, idx:idx + 1], sps[:], axis=AX.X), mark=pV)
                        kb.wait("scalar", pT, m_nq)
                kb.wait("vector", pV, m_red)
                kb.op("vector", lambda e: e.reduce_max(rq[:], rstat[:, 0:4], axis=AX.X))
                m_rk = kb.op("vector", lambda e: e.reduce_max(
                    rk[:], rstat[:, 4:8], axis=AX.X), mark=pV)
                kb.wait("vector", pV, m_rk)
                m_prod = kb.op("vector", lambda e: e.tensor_mul(rq[:], rq[:], rk[:]),
                               mark=pV)
                kb.wait("scalar", pV, m_prod)
                m_sqc = kb.op("scalar", lambda e: e.activation(
                    negc[:], rq[:], AF.Sqrt, scale=CMARG / 128.0), mark=pC)
                kb.wait("vector", pC, m_sqc)
                m_negc = kb.op("vector", lambda e: e.tensor_scalar_mul(
                    negc[:], negc[:], -1.0), mark=pV)
                kb.wait("scalar", pV, m_negc)

                kb.wait("tensor", pV, m_red)
                for qb in range(16):
                    for g in range(qb + 1):
                        epar = g % 2
                        diag = (g == qb)
                        kb.wait("tensor", pC, exp_done)
                        for i in range(4):
                            kc = 4 * g + i
                            fn = lambda e, kc=kc, i=i, qb=qb: e.matmul(
                                sps[:, i * 512:(i + 1) * 512],
                                kth[:, kc * 128:(kc + 1) * 128],
                                qth[:, qb * 512:(qb + 1) * 512],
                                start=True, stop=True)
                            if i == 3:
                                m_s = kb.op("tensor", fn, mark=pT)
                            else:
                                kb.op("tensor", fn)
                        kb.wait("scalar", pT, m_s)
                        kb.wait("scalar", pT, pv_done_for_E[epar])
                        m_exp = kb.op("scalar", lambda e, epar=epar: e.activation(
                            Ebuf[:, epar, :], sps[:], AF.Exp,
                            bias=negc[:], scale=SCALE), mark=pC)
                        exp_done = m_exp
                        if diag:
                            kb.wait("gpsimd", pC, m_exp)
                            for i in range(4):
                                fn = lambda e, epar=epar, i=i: e.affine_select(
                                    Ebuf[:, epar, i * 512:(i + 1) * 512],
                                    Ebuf[:, epar, i * 512:(i + 1) * 512],
                                    pattern=[[1, 512]], compare_op=ALU.is_ge,
                                    fill=0.0, base=-128 * i, channel_multiplier=-1)
                                if i == 3:
                                    m_msk = kb.op("gpsimd", fn, mark=pP)
                                else:
                                    kb.op("gpsimd", fn)
                        kb.wait("tensor", pC, m_exp)
                        if diag:
                            kb.wait("tensor", pP, m_msk)
                        if g == 0:
                            kb.wait("tensor", pV, ops_free)
                        for i in range(4):
                            for sub in range(4):
                                fn = lambda e, epar=epar, i=i, sub=sub, g=g, qb=qb: \
                                    e.matmul(
                                        ops[sub][:, 0:129],
                                        Ebuf[:, epar, i * 512 + sub * 128:
                                             i * 512 + (sub + 1) * 128],
                                        vth[:, 4 * g + i, 0:129],
                                        start=(g == 0 and i == 0),
                                        stop=(g == qb and i == 3))
                                if i == 3 and sub == 3:
                                    m_pv = kb.op("tensor", fn, mark=pT)
                                else:
                                    kb.op("tensor", fn)
                        pv_done_for_E[epar] = m_pv
                    kb.wait("vector", pT, m_pv)
                    if b == 1 and qb == 0 and hl == 0:
                        kb.wait("vector", dS, stage_dma)
                    for sub in range(3):
                        kb.op("vector", lambda e, sub=sub: e.reciprocal(
                            rlq[:, sub:sub + 1], ops[sub][:, 128:129]))
                    m_rl = kb.op("vector", lambda e: e.reciprocal(
                        rlq[:, 3:4], ops[3][:, 128:129]), mark=pV)
                    kb.wait("vector", pV, m_rl)
                    for sub in range(4):
                        fn = lambda e, sub=sub, qb=qb, hl=hl: e.tensor_scalar(
                            stage[:, qb * 4 + sub, hl * 128:(hl + 1) * 128],
                            ops[sub][:, 0:128], rlq[:, sub:sub + 1], None, ALU.mult)
                        if sub == 3:
                            ops_free = kb.op("vector", fn, mark=pV)
                        else:
                            kb.op("vector", fn)
                    st_wr = ops_free

        kb.wait("sync", pV, st_wr)
        stage_dma = kb.dma(
            "sync",
            at_d[b * S:(b + 1) * S, :].rearrange("(p sub) f -> p sub f", p=128),
            stage[:], dS)

    kb.wait("gpsimd", dS, stage_dma)
    kb.op("gpsimd", lambda e: e.collective_compute(
        "AllGather", ALU.bypass, replica_groups=[list(range(W))],
        ins=[at_d.opt()], outs=[at_all.opt()]), mark=cc)
    cc_at = cc["n"]

    kb.barrier(bar)
    if debug:
        kb.dma("sync", o_q, qth[:], dS)
        kb.dma("sync", o_k, kth[:], dS)
        kb.dma("sync", o_vt, vth[:], dS)
        kb.dma("sync", o_stage, stage[:], dS)
        kb.dma("sync", o_negc, negc[:], dS)
        kb.barrier(bar)
    p3.close()

    attnT = st.enter_context(nc.sbuf_tensor("attnT", [128, 16, 2048], BF16))
    xnTo = st.enter_context(nc.sbuf_tensor("xnTo", [128, 16, 2048], BF16))

    t_xnTo = 0
    for ic in range(16):
        for half in range(2):
            t_xnTo = kb.dma(
                "sync", xnTo[:, ic, half * 1024:(half + 1) * 1024],
                xn_d[half * 1024:(half + 1) * 1024, ic * 128:(ic + 1) * 128],
                dS, transpose=True)

    with ExitStack() as pg:
        gbuf = pg.enter_context(nc.sbuf_tensor("gbuf", [128, 2, 256], BF16))
        tpsC = pg.enter_context(nc.psum_tensor("tpsC", [128, 2, 1024], BF16))
        kb.wait("gpsimd", cc, cc_at)
        kb.wait("gpsimd", dS, t_gix)
        g_free = [0, 0]
        tr_rd = [0, 0]
        it = 0
        for j in range(16):
            for r in range(8):
                par = it % 2
                kb.wait("gpsimd", pV, g_free[par])
                t_g = kb.idma(gbuf[:, par, :], at_all, gix[:, j, r:r + 1], dG)
                kb.wait("tensor", dG, t_g)
                kb.wait("tensor", pV, tr_rd[par])
                for u in range(2):
                    fn = lambda e, par=par, u=u: e.transpose(
                        tpsC[:, par, u * 128:(u + 1) * 128],
                        gbuf[:, par, u * 128:(u + 1) * 128], idb[:])
                    if u == 1:
                        m_trc = kb.op("tensor", fn, mark=pT)
                    else:
                        kb.op("tensor", fn)
                kb.wait("vector", pT, m_trc)
                m_ev = kb.op("vector", lambda e, j=j, r=r, par=par: e.tensor_copy(
                    attnT[:, 2 * r:2 * r + 2, j * 128:(j + 1) * 128],
                    tpsC[:, par, 0:256].rearrange("p (k d) -> p k d", k=2)),
                    mark=pV)
                g_free[par] = m_ev
                tr_rd[par] = m_ev
                it += 1
        m_attnT = pV["n"]

    kb.barrier(bar)

    xoT = st.enter_context(nc.sbuf_tensor("xoT", [128, 2048], F32))
    ystg = st.enter_context(nc.sbuf_tensor("ystg", [128, 2048], F32))
    gT = st.enter_context(nc.sbuf_tensor("gT", [128, 2, 2048], BF16))
    alpear = st.enter_context(nc.sbuf_tensor("alpear", [128, 512], F32))

    kb.wait("tensor", dS, t_xnTo)
    kb.wait("tensor", pV, m_attnT)

    y_dma = 0
    silu_done = 0
    comb_done = 0
    gT_rd = [0, 0]
    m_s2 = 0
    m_bw = 0
    for qu in range(4):
        with ExitStack() as pw:
            awf = pw.enter_context(nc.sbuf_tensor(f"awf{qu}", [128, 16, 512], BF16))
            woq = pw.enter_context(nc.sbuf_tensor(f"woq{qu}", [128, 16, 512], BF16))
            with ExitStack() as pwf:
                wfc = pwf.enter_context(nc.sbuf_tensor(f"wfc{qu}", [128, 16, 512], BF16))
                if qu > 0:
                    kb.wait("gpsimd", pT, m_s2)
                t_wf = kb.dma("gpsimd", wfc[:],
                              wf_t[:, qu * 512:(qu + 1) * 512].rearrange(
                                  "(ic p) o -> p ic o", p=128), dG)
                kb.wait("scalar", dG, t_wf)
                if qu > 0:
                    kb.wait("scalar", pT, m_s2)
                m_abs = None
                for ic in range(16):
                    m_abs = kb.op("scalar", lambda e, ic=ic: e.activation(
                        awf[:, ic, :], wfc[:, ic, :], AF.Abs), mark=pC)
                with ExitStack() as palp:
                    aps = palp.enter_context(nc.psum_tensor(f"aps{qu}", [128, 512], F32))
                    kb.wait("tensor", pC, m_abs)
                    if qu > 0:
                        kb.wait("tensor", pV, comb_done)
                    for ic in range(16):
                        fn = lambda e, ic=ic: e.matmul(
                            aps[:], onesb[:], awf[:, ic, :],
                            start=(ic == 0), stop=(ic == 15))
                        if ic == 15:
                            m_aps = kb.op("tensor", fn, mark=pT)
                        else:
                            kb.op("tensor", fn)
                    kb.wait("vector", pT, m_aps)
                    if qu > 0:
                        kb.wait("vector", pV, m_bw)
                    m_alp = kb.op("vector", lambda e: e.tensor_scalar_mul(
                        alpear[:], aps[:], 1.0 / 2048.0), mark=pV)
                kb.wait("vector", pV, m_alp)
                m_bw = None
                for ic in range(16):
                    kb.op("vector", lambda e, ic=ic: e.tensor_single_scalar(
                        awf[:, ic, :], wfc[:, ic, :], 0.0, ALU.is_ge))
                    kb.op("vector", lambda e, ic=ic: e.tensor_scalar(
                        awf[:, ic, :], awf[:, ic, :], 2.0, 1.0,
                        ALU.mult, ALU.subtract))
                    m_bw = kb.op("vector", lambda e, ic=ic: e.tensor_mul(
                        awf[:, ic, :], awf[:, ic, :], alpear[:]), mark=pV)
            bwq = awf
            t_wo = kb.dma("gpsimd", woq[:],
                          wo_t[:, qu * 512:(qu + 1) * 512].rearrange(
                              "(ic p) o -> p ic o", p=128), dG)

            with ExitStack() as pps:
                gps = pps.enter_context(nc.psum_tensor(f"gps{qu}", [128, 2048], F32))
                sps2 = pps.enter_context(nc.psum_tensor(f"sps2_{qu}", [128, 2048], F32))
                for fl in range(4):
                    fc = qu * 4 + fl
                    gpar = fc % 2
                    kb.wait("sync", pV, comb_done)
                    xo_dma = kb.dma("sync", xoT[:],
                                    x_own_t[fc * 128:(fc + 1) * 128, :], dS)
                    kb.wait("tensor", pC, silu_done)
                    if fl == 0:
                        kb.wait("tensor", pV, m_bw)
                    for ic in range(16):
                        for tb in range(4):
                            fn = lambda e, ic=ic, fl=fl, tb=tb: e.matmul(
                                gps[:, tb * 512:(tb + 1) * 512],
                                bwq[:, ic, fl * 128:(fl + 1) * 128],
                                xnTo[:, ic, tb * 512:(tb + 1) * 512],
                                start=(ic == 0), stop=(ic == 15))
                            if ic == 15 and tb == 3:
                                m_g = kb.op("tensor", fn, mark=pT)
                            else:
                                kb.op("tensor", fn)
                    kb.wait("scalar", pT, m_g)
                    kb.wait("scalar", pV, gT_rd[gpar])
                    silu_done = kb.op("scalar", lambda e, fc=fc, gpar=gpar: e.activation(
                        gT[:, gpar, :], gps[:], AF.Silu,
                        bias=fbs[:, fc:fc + 1], scale=1.0), mark=pC)
                    kb.wait("tensor", pV, comb_done)
                    if fl == 0:
                        kb.wait("tensor", dG, t_wo)
                    for ic in range(16):
                        for tb in range(4):
                            fn = lambda e, ic=ic, fl=fl, tb=tb: e.matmul(
                                sps2[:, tb * 512:(tb + 1) * 512],
                                woq[:, ic, fl * 128:(fl + 1) * 128],
                                attnT[:, ic, tb * 512:(tb + 1) * 512],
                                start=(ic == 0), stop=(ic == 15))
                            if ic == 15 and tb == 3:
                                m_s2 = kb.op("tensor", fn, mark=pT)
                            else:
                                kb.op("tensor", fn)
                    kb.wait("vector", pT, m_s2)
                    kb.wait("vector", pC, silu_done)
                    kb.wait("vector", dS, xo_dma)
                    kb.wait("vector", dS, y_dma)
                    kb.op("vector", lambda e, gpar=gpar: e.scalar_tensor_tensor(
                        ystg[:], sps2[:], 1.0, gT[:, gpar, :], ALU.bypass, ALU.mult))
                    comb_done = kb.op("vector", lambda e: e.tensor_add(
                        ystg[:], ystg[:], xoT[:]), mark=pV)
                    gT_rd[gpar] = comb_done
                    kb.wait("sync", pV, comb_done)
                    y_dma = kb.dma("sync", y_t[fc * 128:(fc + 1) * 128, :], ystg[:], dS)

    if debug:
        kb.wait("sync", dS, y_dma)
        kb.dma("sync", o_xnd, xn_d, dS)
        kb.dma("sync", o_xnall, xn_all[4096:4224, :], dS)
        kb.dma("sync", o_atd, at_d[0:128, :], dS)
        kb.dma("sync", o_atall, at_all[2 * TOK + 4096:2 * TOK + 4224, :], dS)
        kb.dma("sync", o_attnT, attnT[:], dS)
        kb.dma("sync", o_xnTo, xnTo[:], dS)
        y_dma = kb.dma("sync", o_gT, gT[:], dS)
    kb.wait("sync", dS, y_dma)

    with nc.Block() as blk:
        kb.emit(blk)
    st.close()
    return nc


_NC_CACHE = None


def _get_nc():
    global _NC_CACHE
    if _NC_CACHE is None:
        _NC_CACHE = build()
    return _NC_CACHE


def make_in_maps(L0_attn_hidden_states, L0_attn_w_qkv, L0_attn_w_o, L0_ffn_weight,
                 L0_ffn_bias):
    x = np.ascontiguousarray(np.asarray(L0_attn_hidden_states, dtype=np.float32)
                             ).reshape(TOK, H)
    wqkv = np.asarray(L0_attn_w_qkv, dtype=np.float32)
    wo = np.asarray(L0_attn_w_o, dtype=np.float32)
    wf = np.asarray(L0_ffn_weight, dtype=np.float32)
    fbv = np.asarray(L0_ffn_bias, dtype=np.float32)

    wo_t = np.ascontiguousarray(wo.T)
    wf_t = np.ascontiguousarray(wf.T)
    fb_l = np.ascontiguousarray(fbv.reshape(16, 128).T)

    in_maps = []
    for c in range(W):
        sl = np.concatenate([
            wqkv[256 * c:256 * c + 256],
            wqkv[H + 256 * c:H + 256 * c + 256],
            wqkv[2 * H + 256 * c:2 * H + 256 * c + 256],
        ], axis=0)
        wq_tc = np.ascontiguousarray(sl.T)
        x_own = np.ascontiguousarray(x[c * CH:(c + 1) * CH])
        x_own_t = np.ascontiguousarray(x_own.T)
        gi = np.zeros((128, 16, 8), dtype=np.int32)
        p = np.arange(128)
        for j in range(16):
            g = c * CH + 128 * j + p
            bb = g // S
            s = g % S
            sub = s // 128
            pp = s % 128
            row = bb * S + pp * 64 + sub
            for r in range(8):
                gi[:, j, r] = r * TOK + row
        in_maps.append({
            "x_own": x_own, "x_own_t": x_own_t, "wq_t": wq_tc,
            "wo_t": wo_t, "wf_t": wf_t, "fb": fb_l, "gidx": gi,
        })

    return in_maps


def kernel(L0_attn_hidden_states, L0_attn_w_qkv, L0_attn_w_o, L0_ffn_weight,
           L0_ffn_bias):
    in_maps = make_in_maps(L0_attn_hidden_states, L0_attn_w_qkv, L0_attn_w_o,
                           L0_ffn_weight, L0_ffn_bias)
    nc = _get_nc()
    res = run_bass_kernel_spmd(nc, in_maps, core_ids=list(range(W)))
    y = np.concatenate([np.asarray(res.results[c]["y_t"]).T for c in range(W)],
                       axis=0)
    return np.ascontiguousarray(y.reshape(B, S, H))
